# version 01 (final)
# baseline (speedup 1.0000x reference)
import os
from contextlib import ExitStack

import numpy as np
import ml_dtypes

import concourse.bass as bass
import concourse.mybir as mybir
from concourse.bass_utils import run_bass_kernel_spmd
from concourse.bass_core_rust import IndirectOffsetOnAxis

F32 = mybir.dt.float32
BF16 = mybir.dt.bfloat16
I32 = mybir.dt.int32
U32 = mybir.dt.uint32
AF = mybir.ActivationFunctionType
ALU = mybir.AluOpType
AXX = mybir.AxisListType.X

PI = float(np.pi)
M23 = float(2 ** 23)
RND = float(1.5 * 2 ** 23)
INV2PI = float(1.0 / (2 * np.pi))
NEG2PI = float(-2 * np.pi)
EPS = 1e-5
D = 1024
DC = 8
NHEAD = 8
TOPK = 10
NKEY = 256
NEXP = 65536
INVDQ = float(1.0 / 0.075)
NGBUF = 10
NCORE = 8
NVT = 50
VH = NVT + 2 + TOPK + 1 + TOPK
S_H = 13
T_H = 2 + 2 * TOPK


class Ctr:
    def __init__(self, sem, name=""):
        self.sem = sem
        self.v = 0
        self.name = name

    def inc(self, inst, n=1):
        inst.then_inc(self.sem, n)
        self.v += n
        return self.v


def bcast(ap, pattern):
    return bass.AP(ap.tensor, ap.offset, [list(ap.ap[0])] + pattern)


def build_program(T):
    NT = T // 512
    NSUB = 4
    nc = bass.Bass("TRN2", target_bir_lowering=False)

    def din(name, shape, dt=F32):
        return nc.dram_tensor(name, shape, dt, kind="ExternalInput").ap()

    dbg = bool(int(os.environ.get("KDBG", "0")))

    def dint(name, shape, dt):
        kind = "ExternalOutput" if (dbg and not name.startswith("cs_")) else "Internal"
        return nc.dram_tensor(name, shape, dt, kind=kind).ap()

    xb = din("xb", [T + 1, D])
    posT = din("posT", [D, T])
    wpv = din("wpv", [D, D]); wk = din("wk", [D, D]); wkv = din("wkv", [D, D])
    wg1 = din("wg1", [2 * D, D]); wg2c = din("wg2c", [128, DC])
    wb1 = din("wb1", [D, 512]); wb2 = din("wb2", [512, 2])
    wo = din("wo", [D, D]); wq = din("wq", [D, 2 * D])
    bpvc = din("bpvc", [128, DC]); bkc = din("bkc", [128, DC]); bkvc = din("bkvc", [128, DC])
    bg1c = din("bg1c", [128, DC]); bg2r = din("bg2r", [1, 1])
    bb1c = din("bb1c", [128, 4]); bb2r = din("bb2r", [2, 1])
    boc = din("boc", [128, DC]); bqc = din("bqc", [128, 16])
    gc_in = din("gc_in", [128, DC]); lnbc = din("lnbc", [128, DC])
    k1t = din("k1t", [128, NKEY]); k2t = din("k2t", [128, NKEY])
    wcomb = din("wcomb", [NEXP, 2 * D], BF16)
    bflag = din("bflag", [128, 1])
    xr = din("xr", [T + 1, D])
    posTr = din("posTr", [D, T])
    rflag = din("rflag", [128, 1])
    metai = din("metai", [1, 2], I32)
    cmask = din("cmask", [128, 4])

    out = nc.dram_tensor("out", [T, D], F32, kind="ExternalOutput").ap()

    wpv_b = dint("wpv_b", [D, D], BF16); wk_b = dint("wk_b", [D, D], BF16)
    wkv_b = dint("wkv_b", [D, D], BF16); wg1_b = dint("wg1_b", [2 * D, D], BF16)
    wb1_b = dint("wb1_b", [D, 512], BF16); wb2_b = dint("wb2_b", [512, 2], BF16)
    wo_b = dint("wo_b", [D, D], BF16); wq_b = dint("wq_b", [D, 2 * D], BF16)
    sc8 = dint("sc8", [8, D, T], BF16)
    rows_sc = dint("rows_sc", [4, T], F32)
    dbg8 = dint("dbg8", [8, D, T], BF16)
    dbgC = dint("dbgC", [128, 40], F32)
    dbgE = dint("dbgE", [4, 128, 40], F32)
    dbgT = dint("dbgT", [2, T, D], BF16)
    dbgQ = dint("dbgQ", [2 * D, T], BF16)
    dbgF = dint("dbgF", [128, 1024], F32)
    dbgU = dint("dbgU", [128, 2048], F32)
    dbgH = dint("dbgH", [8, 128, 48], F32)
    sh_tot = nc.dram_tensor("sh_tot", [2, 2, 128, 40], F32, kind="Internal",
                            addr_space="Shared").ap()
    sh_flag = nc.dram_tensor("sh_flag", [2, 1, 4], I32, kind="Internal",
                             addr_space="Shared").ap()

    ctx = ExitStack()

    def sb(name, shape, dt=F32):
        return ctx.enter_context(nc.sbuf_tensor(name, shape, dt))

    def psum(name, shape, dt=F32):
        return ctx.enter_context(nc.psum_tensor(name, shape, dt))

    def mksem(name):
        return Ctr(ctx.enter_context(nc.semaphore(name)), name)

    qS = mksem("qS"); qG = mksem("qG")
    cV = mksem("cV"); cS = mksem("cS"); cT = mksem("cT"); pse = mksem("pse")

    ps0 = psum("ps0", [128, 512])
    ps1 = psum("ps1", [128, 512])
    ps_up = psum("ps_up", [128, 1024])
    ps_r1 = psum("ps_r1", [128, 512])
    ps_r2 = psum("ps_r2", [128, 512])
    ps_tm = psum("ps_tm", [128, 1024], BF16)
    ps_s = psum("ps_s", [128, 512])

    identity = sb("identity", [128, 128], BF16)
    pcol = sb("pcol", [128, 1])
    ones_col = sb("ones_col", [128, 1], BF16)
    ones1 = sb("ones1", [1, 128], BF16)
    epsc = sb("epsc", [128, 1])
    halfpic = sb("halfpic", [128, 1])
    mask16c = sb("mask16c", [128, 1], I32)
    wq_s = sb("wq_s", [128, 8 * 16 * 128], BF16)
    wo_s = sb("wo_s", [128, 8 * 8 * 128], BF16)
    k1_s = sb("k1_s", [128, NKEY], BF16)
    k2_s = sb("k2_s", [128, NKEY], BF16)
    wg2_s = sb("wg2_s", [128, DC], BF16)
    bias_pv = sb("bias_pv", [128, DC]); bias_k = sb("bias_k", [128, DC])
    bias_kv = sb("bias_kv", [128, DC]); bias_g1 = sb("bias_g1", [128, DC])
    bias_b1 = sb("bias_b1", [128, 4]); bias_q = sb("bias_q", [128, 16])
    bias_g2 = sb("bias_g2", [1, 1]); bias_b2 = sb("bias_b2", [2, 1])
    bo2 = sb("bo2", [128, DC])
    gcol_s = sb("gcol_s", [128, DC]); lnb_s = sb("lnb_s", [128, DC], BF16)
    flag_s = sb("flag_s", [128, 1])
    rflag_s = sb("rflag_s", [128, 1])
    cmask_s = sb("cmask_s", [128, 4])
    meta_sb = sb("meta_sb", [1, 2], I32)
    nonce_sb = sb("nonce_sb", [1, 4], I32)
    fsb = sb("fsb", [1, 4], I32)
    lnst = sb("lnst", [128, NT * NSUB, 2])
    carry_t = sb("carry_t", [128, DC, 4])
    gscar = sb("gscar", [1, 1])
    carry_tR = sb("carry_tR", [128, DC, 4])
    gscarR = sb("gscarR", [1, 1])
    exO = sb("exO", [128, 40])
    exR = sb("exR", [128, 40])
    carryC = sb("carryC", [128, 40])
    lnbF = sb("lnbF", [128, DC], BF16)
    lnbFR = sb("lnbFR", [128, DC], BF16)
    kcB = sb("kcB", [128, DC], BF16); ksB = sb("ksB", [128, DC], BF16)
    kcBR = sb("kcBR", [128, DC], BF16); ksBR = sb("ksBR", [128, DC], BF16)

    def wq_chunk(dc, fc):
        return wq_s[:, (dc * 16 + fc) * 128:(dc * 16 + fc + 1) * 128]

    def wo_chunk(dc, fc):
        return wo_s[:, (dc * 8 + fc) * 128:(dc * 8 + fc + 1) * 128]

    ctxA = ExitStack()

    def sba(name, shape, dt=F32):
        return ctxA.enter_context(nc.sbuf_tensor(name, shape, dt))

    wstage = sba("wstage", [128, 2 * D], BF16)
    wslot = [sba(f"wslot{i}", [128, 64 * 128], BF16) for i in range(2)]

    def wsl(i, dc, fc):
        return wslot[i][:, (dc * 8 + fc) * 128:(dc * 8 + fc + 1) * 128]

    xt = sba("xt", [128, D])
    jk1 = sba("jk1", [128, D])
    ln1tm = sba("ln1tm", [128, D], BF16)
    ln1F = sba("ln1F", [128, DC, 513], BF16)
    stt = sba("stt", [128, 8])
    pvt = sba("pvt", [128, DC, 512], BF16)
    kvt = sba("kvt", [128, DC, 512], BF16)
    kct = sba("kct", [128, DC, 513], BF16)
    kst = sba("kst", [128, DC, 513], BF16)
    ghF = sba("ghF", [128, DC, 512], BF16)
    bhF = sba("bhF", [128, 4, 512], BF16)
    ascr = sba("ascr", [128, 512])
    ascr2 = sba("ascr2", [128, 512])
    ascr3 = sba("ascr3", [128, 512])
    post = sba("post", [128, 512])
    pcc = sba("pcc", [128, 512], BF16)
    psc = sba("psc", [128, 512], BF16)
    ppc = sba("ppc", [128, 512], BF16)
    pps = sba("pps", [128, 512], BF16)
    kkc = sba("kkc", [128, 512], BF16)
    kks = sba("kks", [128, 512], BF16)
    scf = [sba(f"scf{i}", [128, 512]) for i in range(4)]
    stg4 = sba("stg4", [128, 4])
    grow_t = sba("grow_t", [1, 1])
    vgrow = sba("vgrow", [1, 512])
    vgrowb = sba("vgrowb", [1, 512], BF16)
    gsrow = sba("gsrow", [1, 512])
    bw2sb = sba("bw2sb", [2, 512])
    bwrow1 = sba("bwrow1", [1, 512])
    bw0row = sba("bw0row", [1, 512])
    xrow = sba("xrow", [1, D])
    brow = sba("brow", [1, 8])
    lnrow = sba("lnrow", [1, D], BF16)
    bscr = sba("bscr", [1, D])
    kphB = sba("kphB", [128, DC])

    with nc.Block() as blk:
        @blk.sync
        def _(sync):
            for dst, src in [(bias_pv, bpvc), (bias_k, bkc), (bias_kv, bkvc),
                             (bias_g1, bg1c), (bias_b1, bb1c), (bias_q, bqc),
                             (bias_g2, bg2r), (bias_b2, bb2r), (bo2, boc),
                             (gcol_s, gc_in), (flag_s, bflag), (rflag_s, rflag),
                             (cmask_s, cmask), (meta_sb, metai)]:
                qS.inc(sync.dma_start(dst[:], src), 16)

        @blk.gpsimd
        def _(gp):
            qG.inc(gp.dma_start(k1_s[:], k1t), 16)
            qG.inc(gp.dma_start(k2_s[:], k2t), 16)
            qG.inc(gp.dma_start(wg2_s[:], wg2c), 16)
            qG.inc(gp.dma_start(lnb_s[:], lnbc), 16)
            gp.memset(ones_col[:], 1.0)
            gp.memset(ones1[:], 1.0)
            gp.memset(epsc[:], EPS)
            gp.memset(halfpic[:], PI / 2)
            gp.memset(mask16c[:], 0xFFFF)
            gp.memset(gscar[:], 0.0)
            gp.memset(carry_t[:], 0.0)
            gp.memset(gscarR[:], 0.0)
            gp.memset(carry_tR[:], 0.0)

            gp.iota(identity[:], [[1, 128]], base=0, channel_multiplier=0,
                    allow_small_or_imprecise_dtypes=True)
            gp.iota(pcol[:], [[0, 1]], base=0, channel_multiplier=1,
                    allow_small_or_imprecise_dtypes=True)
            gp.wait_ge(qS.sem, qS.v)
            gp.wait_ge(qG.sem, qG.v)

    with nc.Block() as blk:
        @blk.vector
        def _(v):
            cV.inc(v.tensor_scalar(identity[:], identity[:], pcol[:, 0:1], None, ALU.is_equal))

    with nc.Block() as blk:
        @blk.gpsimd
        def _(gp):
            for (src, dst, rows, mode) in [
                    (wpv, wpv_b, D, None), (wk, wk_b, D, None), (wkv, wkv_b, D, None),
                    (wg1, wg1_b, 2 * D, None), (wb1, wb1_b, D, None),
                    (wb2, wb2_b, 512, None), (wq, wq_b, D, None), (wo, wo_b, D, "g")]:
                ncols = src.shape[1]
                for dc in range(rows // 128):
                    qG.inc(gp.dma_start(wstage[:, 0:ncols], src[dc * 128:(dc + 1) * 128, :]), 16)
                    gp.wait_ge(qG.sem, qG.v)
                    if mode == "g":
                        gp.tensor_scalar(wstage[:, 0:ncols], wstage[:, 0:ncols],
                                         gcol_s[:, dc:dc + 1], None, ALU.mult)
                    qG.inc(gp.dma_start(dst[dc * 128:(dc + 1) * 128, :], wstage[:, 0:ncols]), 16)
                    gp.wait_ge(qG.sem, qG.v)

    with nc.Block() as blk:
        @blk.sync
        def _(sync):
            for dc in range(8):
                for fc in range(16):
                    qS.inc(sync.dma_start(wq_chunk(dc, fc),
                                          wq_b[dc * 128:(dc + 1) * 128, fc * 128:(fc + 1) * 128]), 16)
            for dc in range(8):
                for fc in range(8):
                    qS.inc(sync.dma_start(wo_chunk(dc, fc),
                                          wo_b[dc * 128:(dc + 1) * 128, fc * 128:(fc + 1) * 128]), 16)
            sync.wait_ge(qS.sem, qS.v)

    with nc.Block() as blk:
        @blk.tensor
        def _(t):
            for fc in range(8):
                for dc in range(8):
                    cT.inc(t.matmul(ps0[:, fc:fc + 1], wo_chunk(dc, fc),
                                    lnb_s[:, dc:dc + 1], start=(dc == 0), stop=(dc == 7)))

        @blk.vector
        def _(v):
            v.wait_ge(cT.sem, cT.v)
            cV.inc(v.tensor_add(bo2[:], bo2[:], ps0[:, 0:8]))

    def emit_boundary(xsrc, flagcol, lnbF_t, kcB_t, ksB_t):
      with nc.Block() as blk:
        @blk.sync
        def _(sync):
            qS.inc(sync.dma_start(xrow[:], xsrc[0:1, :]), 16)
            sync.wait_ge(qS.sem, qS.v)


        @blk.scalar
        def _(s):
            s.wait_ge(qS.sem, qS.v)
            cS.inc(s.activation(bscr[:], xrow[:], AF.Identity, accum_out=brow[:, 0:1]))
            cS.inc(s.activation(bscr[:], xrow[:], AF.Square, accum_out=brow[:, 1:2]))

        @blk.vector
        def _(v):
            v.wait_ge(cS.sem, cS.v)
            v.tensor_scalar(brow[:, 0:1], brow[:, 0:1], 1.0 / D, None, ALU.mult)
            v.tensor_scalar(brow[:, 1:2], brow[:, 1:2], 1.0 / D, None, ALU.mult)
            cV.inc(v.scalar_tensor_tensor(brow[:, 2:3], brow[:, 0:1], brow[:, 0:1],
                                          brow[:, 1:2], ALU.mult, ALU.subtract))

        @blk.scalar
        def _(s):
            s.wait_ge(cV.sem, cV.v)
            s.activation(brow[:, 3:4], brow[:, 2:3], AF.Ln, scale=-1.0, bias=epsc[0:1, 0:1])
            cS.inc(s.activation(brow[:, 3:4], brow[:, 3:4], AF.Exp, scale=-0.5))

        @blk.vector
        def _(v):
            v.wait_ge(cS.sem, cS.v)
            cV.inc(v.scalar_tensor_tensor(brow[:, 4:5], brow[:, 0:1], -1.0,
                                          brow[:, 3:4], ALU.mult, ALU.mult))

        @blk.scalar
        def _(s):
            s.wait_ge(cV.sem, cV.v)
            cS.inc(s.activation(lnrow[:], xrow[:], AF.Identity,
                                scale=brow[:, 3:4], bias=brow[:, 4:5]))

      with nc.Block() as blk:
        @blk.tensor
        def _(t):
            for c in range(DC):
                cT.inc(t.transpose(ps_tm[:, c * 128:c * 128 + 1],
                                   lnrow[0:1, c * 128:(c + 1) * 128], ones1[0:1, 0:1]))

        @blk.vector
        def _(v):
            v.wait_ge(cT.sem, cT.v)
            cV.inc(v.tensor_scalar(lnbF_t[:], bcast(ps_tm[:], [[128, DC]]),
                                   flagcol, None, ALU.mult))

      for dc in range(8):
        with nc.Block() as blk:
            @blk.sync
            def _(sync, _dc=dc):
                qS.inc(sync.dma_start(wstage[:, 0:1024], wk_b[_dc * 128:(_dc + 1) * 128, :]), 16)
                sync.wait_ge(qS.sem, qS.v)

            @blk.tensor
            def _(t, _dc=dc):
                t.wait_ge(qS.sem, qS.v)
                for fc in range(8):
                    cT.inc(t.matmul(ps1[:, fc:fc + 1], wstage[:, fc * 128:(fc + 1) * 128],
                                    lnbF_t[:, _dc:_dc + 1], start=(_dc == 0), stop=(_dc == 7)))

      with nc.Block() as blk:
        @blk.vector
        def _(v):
            cV.inc(v.tensor_add(kphB[:], ps1[:, 0:8], bias_k[:]))

        @blk.scalar
        def _(s):
            s.wait_ge(cV.sem, cV.v)
            cS.inc(s.activation(kphB[:], kphB[:], AF.Tanh))
            cS.inc(s.activation(ksB_t[:], kphB[:], AF.Sin, scale=PI))
            cS.inc(s.activation(kphB[:], kphB[:], AF.Abs))
            cS.inc(s.activation(kcB_t[:], kphB[:], AF.Sin, scale=-PI, bias=halfpic[:, 0:1]))

        @blk.vector
        def _(v):
            v.wait_ge(cS.sem, cS.v)
            v.tensor_scalar(kcB_t[:], kcB_t[:], flagcol, None, ALU.mult)
            cV.inc(v.tensor_scalar(ksB_t[:], ksB_t[:], flagcol, None, ALU.mult))

    emit_boundary(xb, flag_s[:, 0:1], lnbF, kcB, ksB)
    emit_boundary(xr, rflag_s[:, 0:1], lnbFR, kcBR, ksBR)

    kstop = int(os.environ.get("KSTOP", "9"))

    def stream_weight(sync, slot, wsrc, nfc):
        for dc in range(8):
            for fc in range(nfc):
                qS.inc(sync.dma_start(wsl(slot, dc, fc),
                                      wsrc[dc * 128:(dc + 1) * 128, fc * 128:(fc + 1) * 128]), 16)

    def emit_proj(slot, nfc, rhs_fn, evac_fn, n_mm_per_fc=8):
        with nc.Block() as blk:
            tb0 = cT.v
            sb0 = cS.v

            @blk.tensor
            def _(t):
                for fc in range(nfc):
                    pb = ps0 if fc % 2 == 0 else ps1
                    if fc >= 2:
                        t.wait_ge(cS.sem, sb0 + fc - 1)
                    for dc in range(8):
                        cT.inc(t.matmul(pb[:, 0:512], wsl(slot, dc, fc), rhs_fn(dc),
                                        start=(dc == 0), stop=(dc == 7)))

            @blk.scalar
            def _(s):
                for fc in range(nfc):
                    pb = ps0 if fc % 2 == 0 else ps1
                    s.wait_ge(cT.sem, tb0 + (fc + 1) * 8)
                    evac_fn(s, fc, pb)

    def emit_passA_tile(t_i, xsrc=None, possrc=None, carry=None, gsc=None,
                        lnbF_t=None, kcB_t=None, ksB_t=None, store=True):
        xsrc = xb if xsrc is None else xsrc
        possrc = posT if possrc is None else possrc
        carry = carry_t if carry is None else carry
        gsc = gscar if gsc is None else gsc
        lnbF_t = lnbF if lnbF_t is None else lnbF_t
        kcB_t = kcB if kcB_t is None else kcB_t
        ksB_t = ksB if ksB_t is None else ksB_t
        base = t_i * 512

        for s_i in range(NSUB):
            sub = t_i * NSUB + s_i
            r0 = 1 + base + s_i * 128
            with nc.Block() as blk:
                @blk.sync
                def _(sync):
                    qS.inc(sync.dma_start(xt[:], xsrc[r0:r0 + 128, :]), 16)
                    sync.wait_ge(qS.sem, qS.v)

                @blk.scalar
                def _(s):
                    s.wait_ge(qS.sem, qS.v)
                    cS.inc(s.activation(jk1[:], xt[:], AF.Identity, accum_out=stt[:, 0:1]))
                    cS.inc(s.activation(jk1[:], xt[:], AF.Square, accum_out=stt[:, 1:2]))

                @blk.vector
                def _(v):
                    v.wait_ge(cS.sem, cS.v)
                    v.tensor_scalar(stt[:, 0:1], stt[:, 0:1], 1.0 / D, None, ALU.mult)
                    v.tensor_scalar(stt[:, 1:2], stt[:, 1:2], 1.0 / D, None, ALU.mult)
                    cV.inc(v.scalar_tensor_tensor(stt[:, 2:3], stt[:, 0:1], stt[:, 0:1],
                                                  stt[:, 1:2], ALU.mult, ALU.subtract))

                @blk.scalar
                def _(s):
                    s.wait_ge(cV.sem, cV.v)
                    s.activation(stt[:, 3:4], stt[:, 2:3], AF.Ln, scale=-1.0,
                                 bias=epsc[:, 0:1])
                    cS.inc(s.activation(stt[:, 3:4], stt[:, 3:4], AF.Exp, scale=-0.5))

                @blk.vector
                def _(v):
                    v.wait_ge(cS.sem, cS.v)
                    v.scalar_tensor_tensor(stt[:, 4:5], stt[:, 0:1], -1.0,
                                           stt[:, 3:4], ALU.mult, ALU.mult)
                    if store:
                        v.tensor_copy(lnst[:, sub, 0:1], stt[:, 3:4])
                    cV.inc(v.tensor_copy(lnst[:, sub, 1:2], stt[:, 4:5])
                           if store else v.tensor_copy(stt[:, 7:8], stt[:, 4:5]))

                @blk.scalar
                def _(s):
                    s.wait_ge(cV.sem, cV.v)
                    cS.inc(s.activation(ln1tm[:], xt[:], AF.Identity,
                                        scale=stt[:, 3:4], bias=stt[:, 4:5]))

                @blk.tensor
                def _(t):
                    t.wait_ge(cS.sem, cS.v)
                    for c in range(DC):
                        cT.inc(t.transpose(ps_tm[:, c * 128:(c + 1) * 128],
                                           ln1tm[:, c * 128:(c + 1) * 128], identity[:]))

                @blk.scalar
                def _(s):
                    s.wait_ge(cT.sem, cT.v)
                    for c in range(DC):
                        cS.inc(s.copy(ln1F[:, c, 1 + s_i * 128:1 + (s_i + 1) * 128],
                                      ps_tm[:, c * 128:(c + 1) * 128]))

        with nc.Block() as blk:
            @blk.vector
            def _(v):
                v.tensor_copy(ln1F[:, :, 0:1].rearrange("p a b -> p (a b)"), lnbF_t[:])
                cV.inc(v.tensor_copy(lnbF_t[:], ln1F[:, :, 512:513].rearrange("p a b -> p (a b)")))

        with nc.Block() as blk:
            @blk.sync
            def _(sync):
                stream_weight(sync, 0, wg1[0:0] if False else wg1_b[0:D, :], 8)
                stream_weight(sync, 1, wg1_b[D:2 * D, :], 8)
                sync.wait_ge(qS.sem, qS.v)

        with nc.Block() as blk:
            tb0 = cT.v
            sb0 = cS.v

            @blk.tensor
            def _(t):
                for fc in range(8):
                    pb = ps0 if fc % 2 == 0 else ps1
                    if fc >= 2:
                        t.wait_ge(cS.sem, sb0 + fc - 1)
                    for dc in range(8):
                        cT.inc(t.matmul(pb[:, 0:512], wsl(0, dc, fc), ln1F[:, dc, 1:513],
                                        start=(dc == 0), stop=False))
                    for dc in range(8):
                        cT.inc(t.matmul(pb[:, 0:512], wsl(1, dc, fc), ln1F[:, dc, 0:512],
                                        start=False, stop=(dc == 7)))

            @blk.scalar
            def _(s):
                for fc in range(8):
                    pb = ps0 if fc % 2 == 0 else ps1
                    s.wait_ge(cT.sem, tb0 + (fc + 1) * 16)
                    cS.inc(s.activation(ghF[:, fc, :], pb[:, 0:512], AF.Gelu,
                                        bias=bias_g1[:, fc:fc + 1]))

        with nc.Block() as blk:
            @blk.tensor
            def _(t):
                for dc in range(8):
                    cT.inc(t.matmul(ps_r1[0:1, 0:512], wg2_s[:, dc:dc + 1], ghF[:, dc, :],
                                    start=(dc == 0), stop=(dc == 7)))

            @blk.scalar
            def _(s):
                s.wait_ge(cT.sem, cT.v)
                cS.inc(s.activation(vgrow[:], ps_r1[0:1, 0:512], AF.Sigmoid,
                                    bias=bias_g2[0:1, 0:1]))

            @blk.vector
            def _(v):
                v.wait_ge(cS.sem, cS.v)
                if store:
                    inst = v.tensor_tensor_scan(gsrow[:], vgrow[:], vgrow[:], gsc[0:1, 0:1],
                                                ALU.add, ALU.bypass)
                else:
                    inst = v.tensor_copy(gsrow[0:1, 0:1], vgrow[0:1, 0:1])
                cV.inc(inst)

            @blk.scalar
            def _(s):
                s.wait_ge(cV.sem, cV.v)
                s.copy(vgrowb[:], vgrow[:])
                cS.inc(s.activation(bscr[0:1, 0:512], vgrow[:], AF.Identity,
                                    accum_out=grow_t[:]))

            @blk.vector
            def _(v):
                v.wait_ge(cS.sem, cS.v)
                cV.inc(v.tensor_add(gsc[:], gsc[:], grow_t[:]))

            @blk.tensor
            def _(t):
                t.wait_ge(cS.sem, cS.v)
                cT.inc(t.matmul(ps_s[:, 0:512], ones1[:], vgrowb[:], start=True, stop=True))

            if store:
                @blk.sync
                def _(sync):
                    sync.wait_ge(cV.sem, cV.v)
                    qS.inc(sync.dma_start(rows_sc[0:1, base:base + 512], vgrow[:]), 16)
                    qS.inc(sync.dma_start(rows_sc[1:2, base:base + 512], gsrow[:]), 16)
                    sync.wait_ge(qS.sem, qS.v)

        with nc.Block() as blk:
            @blk.sync
            def _(sync):
                stream_weight(sync, 0, wpv_b, 8)
                stream_weight(sync, 1, wkv_b, 8)
                sync.wait_ge(qS.sem, qS.v)

        emit_proj(0, 8, lambda dc: ln1F[:, dc, 1:513],
                  lambda s, fc, pb: cS.inc(s.activation(pvt[:, fc, :], pb[:, 0:512],
                                                        AF.Identity, bias=bias_pv[:, fc:fc + 1])))
        emit_proj(1, 8, lambda dc: ln1F[:, dc, 1:513],
                  lambda s, fc, pb: cS.inc(s.activation(kvt[:, fc, :], pb[:, 0:512],
                                                        AF.Identity, bias=bias_kv[:, fc:fc + 1])))

        with nc.Block() as blk:
            @blk.sync
            def _(sync):
                stream_weight(sync, 0, wk_b, 8)
                sync.wait_ge(qS.sem, qS.v)

        def k_evac(s, fc, pb):
            s.activation(ascr[:], pb[:, 0:512], AF.Tanh, bias=bias_k[:, fc:fc + 1])
            s.activation(kst[:, fc, 1:513], ascr[:], AF.Sin, scale=PI)
            s.activation(ascr[:], ascr[:], AF.Abs)
            cS.inc(s.activation(kct[:, fc, 1:513], ascr[:], AF.Sin,
                                scale=-PI, bias=halfpic[:, 0:1]))

        emit_proj(0, 8, lambda dc: ln1F[:, dc, 1:513], k_evac)

        with nc.Block() as blk:
            @blk.vector
            def _(v):
                v.tensor_copy(kct[:, :, 0:1].rearrange("p a b -> p (a b)"), kcB_t[:])
                v.tensor_copy(kst[:, :, 0:1].rearrange("p a b -> p (a b)"), ksB_t[:])
                v.tensor_copy(kcB_t[:], kct[:, :, 512:513].rearrange("p a b -> p (a b)"))
                cV.inc(v.tensor_copy(ksB_t[:], kst[:, :, 512:513].rearrange("p a b -> p (a b)")))

        for c in range(DC):
            with nc.Block() as blk:
                @blk.sync
                def _(sync, _c=c):
                    qS.inc(sync.dma_start(post[:], possrc[_c * 128:(_c + 1) * 128,
                                                       base:base + 512]), 16)
                    sync.wait_ge(qS.sem, qS.v)

                @blk.vector
                def _(v, _c=c):
                    v.wait_ge(qS.sem, qS.v)
                    v.tensor_scalar(ascr[:], post[:], INV2PI, None, ALU.mult)
                    v.tensor_scalar(ascr2[:], ascr[:], RND, -RND, ALU.add, ALU.add)
                    v.scalar_tensor_tensor(ascr2[:], ascr2[:], NEG2PI, post[:], ALU.mult, ALU.add)
                    v.tensor_scalar(ascr3[:], ascr[:], 0.25, RND, ALU.add, ALU.add)
                    v.tensor_scalar(ascr3[:], ascr3[:], -RND, None, ALU.add)
                    cV.inc(v.scalar_tensor_tensor(ascr3[:], ascr3[:], NEG2PI, post[:],
                                                  ALU.mult, ALU.add))

                @blk.scalar
                def _(s, _c=c):
                    s.wait_ge(cV.sem, cV.v)
                    s.activation(psc[:], ascr2[:], AF.Sin)
                    cS.inc(s.activation(pcc[:], ascr3[:], AF.Sin, bias=halfpic[:, 0:1]))

                @blk.vector
                def _(v, _c=c):
                    v.wait_ge(cS.sem, cS.v)
                    v.tensor_mul(ppc[:], pcc[:], pvt[:, _c, :])
                    v.tensor_mul(pps[:], psc[:], pvt[:, _c, :])
                    v.tensor_mul(ascr[:], kct[:, _c, 0:512], kvt[:, _c, :])
                    v.tensor_tensor(kkc[:], ascr[:], ps_s[:, 0:512], ALU.mult)
                    v.tensor_mul(ascr[:], kst[:, _c, 0:512], kvt[:, _c, :])
                    inst = v.tensor_tensor(kks[:], ascr[:], ps_s[:, 0:512], ALU.mult)
                    if store:
                        v.tensor_tensor_scan(scf[0][:], ppc[:], ppc[:], carry[:, _c, 0:1],
                                             ALU.add, ALU.bypass)
                        v.tensor_tensor_scan(scf[1][:], pps[:], pps[:], carry[:, _c, 1:2],
                                             ALU.add, ALU.bypass)
                        v.tensor_tensor_scan(scf[2][:], kkc[:], kkc[:], carry[:, _c, 2:3],
                                             ALU.add, ALU.bypass)
                        inst = v.tensor_tensor_scan(scf[3][:], kks[:], kks[:], carry[:, _c, 3:4],
                                                    ALU.add, ALU.bypass)
                    cV.inc(inst)

                @blk.scalar
                def _(s, _c=c):
                    s.wait_ge(cV.sem, cV.v)
                    s.activation(jk1[:, 0:512], ppc[:], AF.Identity, accum_out=stg4[:, 0:1])
                    s.activation(jk1[:, 0:512], pps[:], AF.Identity, accum_out=stg4[:, 1:2])
                    s.activation(jk1[:, 0:512], kkc[:], AF.Identity, accum_out=stg4[:, 2:3])
                    cS.inc(s.activation(jk1[:, 0:512], kks[:], AF.Identity,
                                        accum_out=stg4[:, 3:4]))

                @blk.vector
                def _(v, _c=c):
                    v.wait_ge(cS.sem, cS.v)
                    cV.inc(v.tensor_add(carry[:, _c, 0:4], carry[:, _c, 0:4], stg4[:]))

                if store:
                    @blk.gpsimd
                    def _(gp, _c=c):
                        gp.wait_ge(cV.sem, cV.v)
                        for k in range(4):
                            qG.inc(gp.dma_start(sc8[k, _c * 128:(_c + 1) * 128, base:base + 512],
                                                scf[k][:]), 16)
                        gp.wait_ge(qG.sem, qG.v)

                if store:
                    @blk.sync
                    def _(sync, _c=c):
                        sync.wait_ge(cS.sem, cS.v)
                        qS.inc(sync.dma_start(sc8[4, _c * 128:(_c + 1) * 128, base:base + 512],
                                              pcc[:]), 16)
                        qS.inc(sync.dma_start(sc8[5, _c * 128:(_c + 1) * 128, base:base + 512],
                                              psc[:]), 16)
                        qS.inc(sync.dma_start(sc8[6, _c * 128:(_c + 1) * 128, base:base + 512],
                                              kct[:, _c, 1:513]), 16)
                        qS.inc(sync.dma_start(sc8[7, _c * 128:(_c + 1) * 128, base:base + 512],
                                              kst[:, _c, 1:513]), 16)
                        sync.wait_ge(qS.sem, qS.v)

        if store and os.environ.get("KDBG") == "1":
            with nc.Block() as blk:
                @blk.sync
                def _(sync):
                    for c in range(DC):
                        qS.inc(sync.dma_start(dbg8[0, c * 128:(c + 1) * 128, base:base + 512],
                                              ln1F[:, c, 1:513]), 16)
                        qS.inc(sync.dma_start(dbg8[1, c * 128:(c + 1) * 128, base:base + 512],
                                              pvt[:, c, :]), 16)
                        qS.inc(sync.dma_start(dbg8[2, c * 128:(c + 1) * 128, base:base + 512],
                                              kvt[:, c, :]), 16)
                        qS.inc(sync.dma_start(dbg8[3, c * 128:(c + 1) * 128, base:base + 512],
                                              ghF[:, c, :]), 16)
                    sync.wait_ge(qS.sem, qS.v)

        if not store:
            return
        with nc.Block() as blk:
            @blk.sync
            def _(sync):
                stream_weight(sync, 1, wb1_b, 4)
                for dc in range(4):
                    qS.inc(sync.dma_start(wstage[:, dc * 2:dc * 2 + 2],
                                          wb2_b[dc * 128:(dc + 1) * 128, :]), 16)
                sync.wait_ge(qS.sem, qS.v)

        emit_proj(1, 4, lambda dc: ln1F[:, dc, 1:513],
                  lambda s, fc, pb: cS.inc(s.activation(bhF[:, fc, :], pb[:, 0:512],
                                                        AF.Gelu, bias=bias_b1[:, fc:fc + 1])))

        with nc.Block() as blk:
            @blk.tensor
            def _(t):
                for dc in range(4):
                    cT.inc(t.matmul(ps_r2[0:2, 0:512], wstage[:, dc * 2:dc * 2 + 2],
                                    bhF[:, dc, :], start=(dc == 0), stop=(dc == 3)))

            @blk.scalar
            def _(s):
                s.wait_ge(cT.sem, cT.v)
                cS.inc(s.activation(bw2sb[:], ps_r2[0:2, 0:512], AF.Identity,
                                    bias=bias_b2[:, 0:1]))

            @blk.sync
            def _(sync):
                sync.wait_ge(cS.sem, cS.v)
                qS.inc(sync.dma_start(bwrow1[:], bw2sb[1:2, :]), 16)

            @blk.vector
            def _(v):
                v.wait_ge(qS.sem, qS.v)
                cV.inc(v.tensor_sub(ascr[0:1, :], bw2sb[0:1, :], bwrow1[:]))

            @blk.scalar
            def _(s):
                s.wait_ge(cV.sem, cV.v)
                cS.inc(s.activation(bw0row[:], ascr[0:1, :], AF.Sigmoid))

            @blk.sync
            def _(sync):
                sync.wait_ge(cS.sem, cS.v)
                qS.inc(sync.dma_start(rows_sc[2:3, base:base + 512], bw0row[:]), 16)
                sync.wait_ge(qS.sem, qS.v)

    if kstop >= 2:
        for t_i in range(NT):
            emit_passA_tile(t_i)
        if os.environ.get("KNOLITE", "0") != "1":
            for t_i in range(NT):
                emit_passA_tile(t_i, xsrc=xr, possrc=posTr, carry=carry_tR, gsc=gscarR,
                                lnbF_t=lnbFR, kcB_t=kcBR, ksB_t=ksBR, store=False)

    if kstop < 3:
        class _Skip(Exception):
            pass
        def _finish():
            with nc.Block() as blk:
                @blk.sync
                def _(sync):
                    sync.wait_ge(qS.sem, qS.v)
                    sync.wait_ge(qG.sem, qG.v)
            ctxA.close()
            ctx.close()
            return nc
        return _finish()

    with nc.Block() as blk:
        @blk.sync
        def _(sync):
            sync.wait_ge(qS.sem, qS.v)

        @blk.vector
        def _(v):
            for j in range(4):
                v.tensor_copy(nonce_sb[:, j:j + 1], meta_sb[0:1, 1:2])
            v.memset(exO[:], 0.0)
            v.memset(exR[:], 0.0)
            v.tensor_copy(exO[:, 0:32], carry_t[:].rearrange("p a b -> p (a b)"))
            v.tensor_copy(exO[0:1, 32:33], gscar[:])
            v.tensor_copy(exR[:, 0:32], carry_tR[:].rearrange("p a b -> p (a b)"))
            cV.inc(v.tensor_copy(exR[0:1, 32:33], gscarR[:]))

    with nc.Block() as blk:
        @blk.sync
        def _(sync):
            rhalf = nc.sync.register("rhalf2").__enter__()
            sync.reg_load(rhalf, meta_sb[0:1, 0:1])
            sync.wait_ge(cV.sem, cV.v)
            with nc.sync.If_eq(rhalf, 0):
                sync.dma_start(sh_tot[0, 0], exO[:]).then_inc(qS.sem, 16)
                sync.dma_start(sh_tot[0, 1], exR[:]).then_inc(qS.sem, 16)
                sync.wait_ge(qS.sem, qS.v + 32)
                sync.dma_start(sh_flag[0], nonce_sb[:]).then_inc(qS.sem, 16)
                sync.wait_ge(qS.sem, qS.v + 48)
            with nc.sync.Else():
                sync.dma_start(sh_tot[1, 0], exO[:]).then_inc(qS.sem, 16)
                sync.dma_start(sh_tot[1, 1], exR[:]).then_inc(qS.sem, 16)
                sync.wait_ge(qS.sem, qS.v + 32)
                sync.dma_start(sh_flag[1], nonce_sb[:]).then_inc(qS.sem, 16)
                sync.wait_ge(qS.sem, qS.v + 48)
            qS.v += 48
            with nc.sync.register("rgo") as rgo, nc.sync.register("rexp") as rexp, \
                 nc.sync.register("rnon") as rnon:
                sync.reg_load(rnon, meta_sb[0:1, 1:2])
                sync.reg_mov(rgo, 1)
                sync.reg_mov(rexp, 0)
                with nc.sync.While(rgo):
                    with nc.sync.If_eq(rhalf, 0):
                        sync.dma_start(fsb[:], sh_flag[1]).then_inc(pse.sem, 16)
                    with nc.sync.Else():
                        sync.dma_start(fsb[:], sh_flag[0]).then_inc(pse.sem, 16)
                    sync.reg_alu(rexp, rexp, 16, ALU.add)
                    sync.wait_ge(pse.sem, rexp)
                    sync.reg_load(rgo, fsb[0:1, 0:1])
                    sync.reg_alu(rgo, rgo, rnon, ALU.not_equal)
            with nc.sync.If_eq(rhalf, 0):
                sync.dma_start(exO[:], sh_tot[1, 0]).then_inc(qS.sem, 16)
                sync.dma_start(exR[:], sh_tot[1, 1]).then_inc(qS.sem, 16)
                sync.wait_ge(qS.sem, qS.v + 32)
            with nc.sync.Else():
                sync.dma_start(exO[:], sh_tot[0, 0]).then_inc(qS.sem, 16)
                sync.dma_start(exR[:], sh_tot[0, 1]).then_inc(qS.sem, 16)
                sync.wait_ge(qS.sem, qS.v + 32)
            qS.v += 32

    with nc.Block() as blk:
        @blk.vector
        def _(v):
            v.memset(carryC[:], 0.0)
            v.scalar_tensor_tensor(carryC[:], exO[:], cmask_s[:, 0:1], carryC[:],
                                   ALU.mult, ALU.add)
            v.scalar_tensor_tensor(carryC[:, 0:32], carry_tR[:].rearrange("p a b -> p (a b)"),
                                   cmask_s[:, 1:2], carryC[:, 0:32], ALU.mult, ALU.add)
            v.scalar_tensor_tensor(carryC[0:1, 32:33], gscarR[:], cmask_s[0:1, 1:2],
                                   carryC[0:1, 32:33], ALU.mult, ALU.add)
            cV.inc(v.scalar_tensor_tensor(carryC[:], exR[:], cmask_s[:, 2:3], carryC[:],
                                          ALU.mult, ALU.add))

        if os.environ.get("KDBG") == "1":
            @blk.sync
            def _(sync):
                sync.wait_ge(cV.sem, cV.v)
                qS.inc(sync.dma_start(dbgC, carryC[:]), 16)
                qS.inc(sync.dma_start(dbgE[0][:, 0:32], carry_t[:].rearrange("p a b -> p (a b)")), 16)
                qS.inc(sync.dma_start(dbgE[1], exO[:]), 16)
                qS.inc(sync.dma_start(dbgE[2][:, 0:32], carry_tR[:].rearrange("p a b -> p (a b)")), 16)
                qS.inc(sync.dma_start(dbgE[3], exR[:]), 16)
                sync.wait_ge(qS.sem, qS.v)

    ctxA.close()

    def sbb(name, shape, dt=F32):
        return ctx.enter_context(nc.sbuf_tensor(name, shape, dt))

    rlc = [sbb(f"rlc{i}", [128, 8, 512], BF16) for i in range(2)]
    blg = sbb("blg", [128, 2, 2 * D], BF16)
    bl = blg.ap().rearrange("p a b -> p (a b)").rearrange("p (c t) -> p c t", c=DC)
    nrmg = sbb("nrmg", [128, 2, 2 * D], BF16)
    nrm = nrmg.ap().rearrange("p a b -> p (a b)").rearrange("p (c t) -> p c t", c=DC)
    attF = sbb("attF", [128, DC, 512], BF16)
    ln1Fb = sbb("ln1Fb", [128, DC, 512], BF16)
    bscr1 = sbb("bscr1", [128, 512], BF16)
    bscr2 = sbb("bscr2", [128, 512], BF16)
    bsq = [sbb(f"bsq{i}", [128, 512], BF16) for i in range(2)]
    growA = sbb("growA", [1, 512]); growB = sbb("growB", [1, 512])
    growC = sbb("growC", [1, 512]); growD = sbb("growD", [1, 512])
    growAb = sbb("growAb", [1, 512], BF16); growBb = sbb("growBb", [1, 512], BF16)
    xb16 = sbb("xb16", [128, NSUB, D], BF16)
    x1t = sbb("x1t", [128, NSUB, D], BF16)
    ln2tm = sbb("ln2tm", [128, NSUB, D], BF16)
    ln2F = sbb("ln2F", [128, DC, 512], BF16)
    qF = sbb("qF", [128, 16, 512], BF16)
    colsc = sbb("colsc", [128, 8])
    jkB = sbb("jkB", [128, D])
    srows = sbb("srows", [128, 512])
    m1a = sbb("m1a", [128, 8]); m1b = sbb("m1b", [128, 8])
    m2a = sbb("m2a", [128, 8]); m2b = sbb("m2b", [128, 8])
    mr1 = sbb("mr1", [128, NKEY]); mr2 = sbb("mr2", [128, NKEY])
    i1a = sbb("i1a", [128, 8], U32); i1b = sbb("i1b", [128, 8], U32)
    i2a = sbb("i2a", [128, 8], U32); i2b = sbb("i2b", [128, 8], U32)
    v1c = sbb("v1c", [128, TOPK]); v2c = sbb("v2c", [128, TOPK])
    i1c = sbb("i1c", [128, TOPK], U32); i2c = sbb("i2c", [128, TOPK], U32)
    i1s = sbb("i1s", [128, TOPK], U32)
    cgrid = sbb("cgrid", [128, 100]); cbuf = sbb("cbuf", [128, 100])
    idg = sbb("idg", [128, 100], U32)
    zg = sbb("zg", [128, 100]); zbuf = sbb("zbuf", [128, 100])
    cq = sbb("cq", [128, 100])
    mAx = sbb("mAx", [128, 8]); mBx = sbb("mBx", [128, 8])
    zAx = sbb("zAx", [128, 8]); zBx = sbb("zBx", [128, 8])
    v10 = sbb("v10", [128, TOPK])
    z10 = mr2[:, 0:96]
    zt = mr1[:, 0:96]
    idsf = mr1[:, 96:192]
    idu = mr2[:, 96:192].bitcast(U32)
    m0c = sbb("m0c", [128, 1]); pcolq = sbb("pcolq", [128, 1]); nm0 = sbb("nm0", [128, 1])
    wexp = sbb("wexp", [128, TOPK]); wsumc = sbb("wsumc", [128, 1])
    wrec = sbb("wrec", [128, 1])
    wts = sbb("wts", [128, TOPK])
    hten = sbb("hten", [128, TOPK]); gelh = sbb("gelh", [128, TOPK])
    hp = sbb("hp", [128, TOPK])
    gbuf = [sbb(f"gbuf{i}", [128, 2 * D], BF16) for i in range(8)]
    gbuf.append(blg[:, 0, :])
    gbuf.append(blg[:, 1, :])
    prodb = [nrmg[:, 1, 0:D], nrmg[:, 1, D:2 * D]]
    diagb = [sbb(f"diagb{i}", [128, 128], BF16) for i in range(2)]
    outt = nrmg[:, 0, :].bitcast(F32)

    with nc.Block() as blk:
        @blk.gpsimd
        def _(gp):
            for _g in gbuf:
                gp.memset(_g[:], 0.0)

    def emit_topk(v):
        s1 = srows[:, 0:256]
        s2 = srows[:, 256:512]
        n0 = cV.v
        cV.inc(v.max(out=m1a[:], in_=s1))
        cV.inc(v.max(out=m2a[:], in_=s2))
        cV.inc(v.tensor_copy(cgrid[:, 0:1], s1[:, 0:1]))
        cV.inc(v.max_index(out=i1a[:], in_max=m1a[:], in_values=s1))
        cV.inc(v.max_index(out=i2a[:], in_max=m2a[:], in_values=s2))
        cV.inc(v.match_replace(out=mr1[:], in_to_replace=m1a[:], in_values=s1,
                               imm_value=-1e30))
        cV.inc(v.match_replace(out=mr2[:], in_to_replace=m2a[:], in_values=s2,
                               imm_value=-1e30))
        cV.inc(v.tensor_copy(v1c[:, 0:8], m1a[:]))
        cV.inc(v.tensor_copy(v2c[:, 0:8], m2a[:]))
        cV.inc(v.tensor_copy(i1c[:, 0:8], i1a[:]))
        cV.inc(v.tensor_copy(i2c[:, 0:8], i2a[:]))
        cV.inc(v.max(out=m1b[:], in_=mr1[:]))
        cV.inc(v.max(out=m2b[:], in_=mr2[:]))
        cV.inc(v.tensor_scalar(i1s[:, 0:8], i1c[:, 0:8], 256.0, None, ALU.mult))
        cV.inc(v.tensor_add(m0c[:], v1c[:, 0:1], v2c[:, 0:1]))
        cV.inc(v.max_index(out=i1b[:], in_max=m1b[:], in_values=mr1[:]))
        cV.inc(v.max_index(out=i2b[:], in_max=m2b[:], in_values=mr2[:]))
        cV.inc(v.tensor_scalar(pcolq[:], m0c[:], -INVDQ, 255.0, ALU.mult, ALU.add))
        cV.inc(v.tensor_scalar(nm0[:], m0c[:], -1.0, None, ALU.mult))
        cV.inc(v.tensor_copy(v1c[:, 8:10], m1b[:, 0:2]))
        cV.inc(v.tensor_copy(v2c[:, 8:10], m2b[:, 0:2]))
        cV.inc(v.tensor_copy(i1c[:, 8:10], i1b[:, 0:2]))
        cV.inc(v.tensor_copy(i2c[:, 8:10], i2b[:, 0:2]))
        cV.inc(v.tensor_scalar(i1s[:, 8:10], i1c[:, 8:10], 256.0, None, ALU.mult))
        cV.inc(v.tensor_tensor(cgrid[:].rearrange("p (a b) -> p a b", a=TOPK),
                               bcast(v1c[:], [[1, TOPK], [0, TOPK]]),
                               bcast(v2c[:], [[0, TOPK], [1, TOPK]]), ALU.add))
        cV.inc(v.tensor_tensor(idg[:].rearrange("p (a b) -> p a b", a=TOPK),
                               bcast(i1s[:], [[1, TOPK], [0, TOPK]]),
                               bcast(i2c[:], [[0, TOPK], [1, TOPK]]), ALU.add))
        cV.inc(v.scalar_tensor_tensor(cq[:], cgrid[:], INVDQ,
                                      bcast(pcolq[:], [[0, 100]]), ALU.mult, ALU.add))
        cV.inc(v.max(out=mAx[:], in_=cgrid[:]))
        cV.inc(v.tensor_scalar(cq[:], cq[:], 0.0, 255.0, ALU.max, ALU.min))
        cV.inc(v.tensor_scalar(cq[:], cq[:], RND, -RND, ALU.add, ALU.add))
        cV.inc(v.match_replace(out=cbuf[:], in_to_replace=mAx[:], in_values=cgrid[:],
                               imm_value=-1e30))
        cV.inc(v.scalar_tensor_tensor(zg[:], cq[:], 65536.0, idg[:], ALU.mult, ALU.add))
        cV.inc(v.tensor_copy(v10[:, 0:8], mAx[:]))
        cV.inc(v.max(out=zAx[:], in_=zg[:]))
        cV.inc(v.max(out=mBx[:], in_=cbuf[:]))
        cV.inc(v.tensor_copy(cgrid[:, 0:8], v1c[:, 0:8]))
        cV.inc(v.match_replace(out=zbuf[:], in_to_replace=zAx[:], in_values=zg[:],
                               imm_value=-1.0))
        cV.inc(v.tensor_copy(v10[:, 8:10], mBx[:, 0:2]))
        cV.inc(v.tensor_scalar(z10[:, 0:8], zAx[:], -32768.0, None, ALU.add))
        cV.inc(v.max(out=zBx[:], in_=zbuf[:]))
        cV.inc(v.tensor_copy(cbuf[:, 0:8], v2c[:, 0:8]))
        cV.inc(v.tensor_copy(cgrid[:, 8:16], v1c[:, 0:8]))
        cV.inc(v.tensor_scalar(z10[:, 8:10], zBx[:, 0:2], -32768.0, None, ALU.add))
        cV.inc(v.tensor_copy(cbuf[:, 8:16], v2c[:, 0:8]))
        cV.inc(v.tensor_copy(cgrid[:, 16:24], v1c[:, 0:8]))
        cV.inc(v.tensor_scalar(zt[:], z10[:], 1.0 / 65536.0, RND, ALU.mult, ALU.add))
        cV.inc(v.tensor_scalar(zt[:], zt[:], -RND, None, ALU.add))
        cV.inc(v.scalar_tensor_tensor(idsf[:], zt[:], -65536.0, z10[:], ALU.mult, ALU.add))
        cV.inc(v.tensor_scalar(idu[:].bitcast(F32), idsf[:], M23 + 32768.0, None, ALU.add))
        cV.inc(v.tensor_scalar(idu[:], idu[:], mask16c[:, 0:1], None, ALU.bitwise_and))
        assert cV.v - n0 == NVT, (cV.v - n0, NVT)

    def emit_ffn_sub(t_i, s_i):
        base = t_i * 512
        with nc.Block() as blk:
            vb = cV.v
            sb_ = cS.v
            tb = cT.v
            gb = qG.v

            @blk.tensor
            def _(t):
                for h in range(NHEAD):
                    if h >= 1:
                        t.wait_ge(cS.sem, sb_ + (h - 1) * S_H + 1)
                    cT.inc(t.matmul(ps_s[:, 0:256], qF[:, 2 * h, s_i * 128:(s_i + 1) * 128],
                                    k1_s[:], start=True, stop=True))
                    cT.inc(t.matmul(ps_s[:, 256:512], qF[:, 2 * h + 1, s_i * 128:(s_i + 1) * 128],
                                    k2_s[:], start=True, stop=True))
                    for k in range(TOPK):
                        t.wait_ge(cV.sem, vb + h * VH + NVT + 2 + TOPK + 1 + (k + 1))
                        g = gbuf[(h * TOPK + k) % NGBUF]
                        first = (h == 0 and k == 0)
                        last = (h == NHEAD - 1 and k == TOPK - 1)
                        cT.inc(t.matmul(ps_up[:, 0:512], diagb[k % 2][:], g[:, D:D + 512],
                                        start=first, stop=last))
                        cT.inc(t.matmul(ps_up[:, 512:1024], diagb[k % 2][:], g[:, D + 512:2 * D],
                                        start=first, stop=(last)))

            @blk.scalar
            def _(s):
                for h in range(NHEAD):
                    s.wait_ge(cT.sem, tb + h * T_H + 2)
                    if h >= 1:
                        s.wait_ge(cV.sem, vb + (h - 1) * VH + NVT)
                    cS.inc(s.copy(srows[:], ps_s[:, 0:512]))
                    s.wait_ge(cV.sem, vb + h * VH + NVT)
                    s.activation(wexp[:], v10[:], AF.Exp, bias=nm0[:, 0:1],
                                 accum_out=wsumc[:])
                    s.activation(wrec[:], wsumc[:], AF.Ln)
                    cS.inc(s.activation(wrec[:], wrec[:], AF.Exp, scale=-1.0))
                    for k in range(TOPK):
                        s.wait_ge(cV.sem, vb + h * VH + NVT + 2 + (k + 1))
                        cS.inc(s.activation(jkB[:], prodb[k % 2][:], AF.Identity,
                                            accum_out=hten[:, k:k + 1]))
                    cS.inc(s.activation(gelh[:], hten[:], AF.Gelu))

            @blk.vector
            def _(v):
                for h in range(NHEAD):
                    v.wait_ge(cS.sem, sb_ + h * S_H + 1)
                    emit_topk(v)
                    v.wait_ge(cS.sem, sb_ + h * S_H + 2)
                    cV.inc(v.tensor_copy(jkB[0:1, 0:1], wsumc[0:1, 0:1]))
                    cV.inc(v.tensor_tensor(wts[:], wexp[:], bcast(wrec[:], [[0, TOPK]]),
                                           ALU.mult))
                    for k in range(TOPK):
                        v.wait_ge(qG.sem, gb + (h * TOPK + min(TOPK, k + 3)) * 16)
                        if k >= 2:
                            v.wait_ge(cS.sem, sb_ + h * S_H + 2 + (k - 1))
                        cV.inc(v.tensor_mul(prodb[k % 2][:],
                                            gbuf[(h * TOPK + k) % NGBUF][:, 0:D],
                                            ln2tm[:, s_i, :]))
                    v.wait_ge(cS.sem, sb_ + (h + 1) * S_H)
                    cV.inc(v.tensor_mul(hp[:], gelh[:], wts[:]))
                    v.tensor_copy(cgrid[:, 0:8], v1c[:, 0:8])
                    v.tensor_copy(cbuf[:, 0:8], v2c[:, 0:8])
                    for k in range(TOPK):
                        if k >= 2:
                            t_target = tb + h * T_H + 2 + 2 * (k - 1)
                            v.wait_ge(cT.sem, t_target)
                        cV.inc(v.tensor_tensor(diagb[k % 2][:], identity[:],
                                               bcast(hp[:, k:k + 1], [[0, 128]]), ALU.mult))

            @blk.gpsimd
            def _(gp):
                for h in range(NHEAD):
                    gp.wait_ge(cV.sem, vb + h * VH + NVT)
                    if dbg and t_i == 0 and s_i == 0 and h >= 1:
                        gp.dma_start(dbgH[h - 1, :, 0:10], hp[:]).then_inc(pse.sem, 16)
                        gp.dma_start(dbgH[h - 1, :, 10:20], wts[:]).then_inc(pse.sem, 16)
                        gp.dma_start(dbgH[h - 1, :, 24:34], hten[:]).then_inc(pse.sem, 16)
                        gp.dma_start(dbgH[h - 1, :, 34:44], gelh[:]).then_inc(pse.sem, 16)
                        with nc.allow_non_contiguous_dma(reason="debug"):
                            gp.dma_start(dbgH[h - 1, :, 20:21], wsumc[:]).then_inc(pse.sem, 16)
                            gp.dma_start(dbgH[h - 1, :, 21:22], wrec[:]).then_inc(pse.sem, 16)
                        pse.v += 96
                    for k in range(TOPK):
                        m = h * TOPK + k
                        if m >= NGBUF:
                            hh, kk = divmod(m - NGBUF, TOPK)
                            gp.wait_ge(cT.sem, tb + hh * T_H + 2 + (kk + 1) * 2)
                        qG.inc(gp.indirect_dma_start(
                            gbuf[m % NGBUF][:], None, wcomb,
                            IndirectOffsetOnAxis(idu[:, k:k + 1], 0)), 16)

            tb_end = tb + NHEAD * T_H
            vb_end = vb + NHEAD * VH
            sb_end = sb_ + NHEAD * S_H

            @blk.scalar
            def _(s):
                s.wait_ge(cT.sem, tb_end)
                s.activation(jkB[:], ps_up[:], AF.Identity, accum_out=colsc[:, 0:1])
                cS.inc(s.activation(jkB[:], ps_up[:], AF.Square, accum_out=colsc[:, 1:2]))

            @blk.vector
            def _(v):
                cV.inc(v.tensor_copy(jkB[0:1, 1:2], ones1[0:1, 0:1]))
                v.wait_ge(cS.sem, sb_end + 1)
                v.tensor_scalar(colsc[:, 0:2], colsc[:, 0:2], 1.0 / D, None, ALU.mult)
                v.tensor_copy(srows[:, 0:8], srows[:, 8:16])
                v.tensor_copy(srows[:, 8:16], srows[:, 0:8])
                cV.inc(v.scalar_tensor_tensor(colsc[:, 2:3], colsc[:, 0:1], colsc[:, 0:1],
                                              colsc[:, 1:2], ALU.mult, ALU.subtract))

            @blk.scalar
            def _(s):
                s.wait_ge(cV.sem, vb_end + 2)
                s.activation(colsc[:, 3:4], colsc[:, 2:3], AF.Ln, scale=-1.0,
                             bias=epsc[:, 0:1])
                cS.inc(s.activation(colsc[:, 3:4], colsc[:, 3:4], AF.Exp, scale=-0.5))

            @blk.vector
            def _(v):
                v.wait_ge(cS.sem, sb_end + 2)
                cV.inc(v.scalar_tensor_tensor(colsc[:, 4:5], colsc[:, 0:1], -1.0,
                                              colsc[:, 3:4], ALU.mult, ALU.mult))

            @blk.scalar
            def _(s):
                s.wait_ge(cV.sem, vb_end + 3)
                cS.inc(s.activation(jkB[:], ps_up[:], AF.Identity,
                                    scale=colsc[:, 3:4], bias=colsc[:, 4:5]))

            @blk.vector
            def _(v):
                v.wait_ge(cS.sem, sb_end + 3)
                cV.inc(v.tensor_add(outt[:], jkB[:], x1t[:, s_i, :]))

            @blk.sync
            def _(sync):
                sync.wait_ge(cV.sem, vb_end + 4)
                r0 = base + s_i * 128
                qS.inc(sync.dma_start(out[r0:r0 + 128, :], outt[:]), 16)
                sync.wait_ge(qS.sem, qS.v)

        if dbg and t_i == 0 and s_i == 0:
            with nc.Block() as blk:
                @blk.vector
                def _(v):
                    cV.inc(v.tensor_copy(jkB[:], ps_up[:]))

                @blk.sync
                def _(sync):
                    sync.wait_ge(cV.sem, cV.v)
                    sync.wait_ge(cS.sem, cS.v)
                    qS.inc(sync.dma_start(dbgF[:, 0:512], srows[:]), 16)
                    qS.inc(sync.dma_start(dbgF[:, 512:522], v10[:]), 16)
                    qS.inc(sync.dma_start(dbgF[:, 522:532], wts[:]), 16)
                    qS.inc(sync.dma_start(dbgF[:, 532:542], hten[:]), 16)
                    qS.inc(sync.dma_start(dbgF[:, 542:552], gelh[:]), 16)
                    qS.inc(sync.dma_start(dbgF[:, 552:562], idu[:, 0:10].bitcast(F32)), 16)
                    with nc.allow_non_contiguous_dma(reason="debug single cols"):
                        qS.inc(sync.dma_start(dbgF[:, 562:563], m0c[:]), 16)
                        qS.inc(sync.dma_start(dbgF[:, 563:564], wsumc[:]), 16)
                    qS.inc(sync.dma_start(dbgF[:, 564:574], z10[:, 0:10]), 16)
                    qS.inc(sync.dma_start(dbgF[:, 574:584], hp[:]), 16)
                    qS.inc(sync.dma_start(dbgF[:, 584:592], zAx[:]), 16)
                    qS.inc(sync.dma_start(dbgF[:, 592:600], zBx[:]), 16)
                    qS.inc(sync.dma_start(dbgF[:, 600:610], zt[:, 0:10]), 16)
                    qS.inc(sync.dma_start(dbgF[:, 610:620], idsf[:, 0:10]), 16)
                    qS.inc(sync.dma_start(dbgF[:, 620:720], cq[:]), 16)
                    qS.inc(sync.dma_start(dbgF[:, 720:820], zg[:]), 16)
                    qS.inc(sync.dma_start(dbgF[:, 820:920], idg[:].bitcast(F32)), 16)
                    qS.inc(sync.dma_start(dbgF[:, 920:930], i1c[:].bitcast(F32)), 16)
                    qS.inc(sync.dma_start(dbgF[:, 930:940], i2c[:].bitcast(F32)), 16)
                    qS.inc(sync.dma_start(dbgF[:, 940:1040 - 100 + 60], cgrid[:, 0:0 + 0]), 0) if False else None
                    qS.inc(sync.dma_start(dbgF[:, 940:948], m1a[:]), 16)
                    qS.inc(sync.dma_start(dbgF[:, 948:956], m2a[:]), 16)
                    qS.inc(sync.dma_start(dbgF[:, 956:964], colsc[:]), 16)
                    qS.inc(sync.dma_start(dbgH[7, :, 0:10], hp[:]), 16)
                    qS.inc(sync.dma_start(dbgH[7, :, 10:20], wts[:]), 16)
                    qS.inc(sync.dma_start(dbgH[7, :, 24:34], hten[:]), 16)
                    qS.inc(sync.dma_start(dbgH[7, :, 34:44], gelh[:]), 16)
                    sync.wait_ge(pse.sem, pse.v)
                    with nc.allow_non_contiguous_dma(reason="debug"):
                        qS.inc(sync.dma_start(dbgH[7, :, 20:21], wsumc[:]), 16)
                        qS.inc(sync.dma_start(dbgH[7, :, 21:22], wrec[:]), 16)
                    qS.inc(sync.dma_start(dbgU[:, 0:1024], jkB[:]), 16)
                    qS.inc(sync.dma_start(dbgU[:, 1024:2048], outt[:]), 16)
                    sync.wait_ge(qS.sem, qS.v)

    def emit_passB_tile(t_i):
        base = t_i * 512

        with nc.Block() as blk:
            @blk.gpsimd
            def _(gp):
                for s_i in range(NSUB):
                    r0 = 1 + base + s_i * 128
                    qG.inc(gp.dma_start(xb16[:, s_i, :], xb[r0:r0 + 128, :]), 16)
                gp.wait_ge(qG.sem, qG.v)

            @blk.sync
            def _(sync):
                qS.inc(sync.dma_start(growC[:], rows_sc[1:2, base:base + 512]), 16)
                qS.inc(sync.dma_start(growD[:], rows_sc[2:3, base:base + 512]), 16)
                sync.wait_ge(qS.sem, qS.v)

        with nc.Block() as blk:
            @blk.vector
            def _(v):
                v.tensor_scalar(growA[:], growC[:], carryC[0:1, 32:33], 1.0, ALU.add, ALU.max)
                cV.inc(v.tensor_copy(growC[:], growA[:]))

            @blk.scalar
            def _(s):
                s.wait_ge(cV.sem, cV.v)
                s.activation(growB[:], growA[:], AF.Ln)
                cS.inc(s.activation(growB[:], growB[:], AF.Exp, scale=-0.5))

            @blk.vector
            def _(v):
                v.wait_ge(cS.sem, cS.v)
                v.tensor_scalar(growA[:], growD[:], -1.0, 1.0, ALU.mult, ALU.add)
                v.scalar_tensor_tensor(growA[:], growA[:], 1.0 / 32.0, growB[:],
                                       ALU.mult, ALU.mult)
                cV.inc(v.tensor_scalar(growB[:], growD[:], 1.0 / 32.0, None, ALU.mult))

            @blk.scalar
            def _(s):
                s.wait_ge(cV.sem, cV.v)
                s.copy(growAb[:], growA[:])
                cS.inc(s.copy(growBb[:], growB[:]))

            @blk.tensor
            def _(t):
                t.wait_ge(cS.sem, cS.v)
                cT.inc(t.matmul(ps_r1[:, 0:512], ones1[:], growBb[:], start=True, stop=True))
                cT.inc(t.matmul(ps_r2[:, 0:512], ones1[:], growAb[:], start=True, stop=True))

        for c in range(DC):
            with nc.Block() as blk:
                @blk.sync
                def _(sync, _c=c):
                    b = rlc[_c % 2]
                    for k in range(8):
                        qS.inc(sync.dma_start(b[:, k, :],
                                              sc8[k, _c * 128:(_c + 1) * 128,
                                                  base:base + 512]), 16)
                    sync.wait_ge(qS.sem, qS.v)

                @blk.vector
                def _(v, _c=c):
                    b = rlc[_c % 2]
                    v.wait_ge(qS.sem, qS.v)
                    v.scalar_tensor_tensor(bscr1[:], b[:, 0, :], carryC[:, _c * 4 + 0:_c * 4 + 1],
                                           b[:, 4, :], ALU.add, ALU.mult)
                    v.scalar_tensor_tensor(bscr2[:], b[:, 1, :], carryC[:, _c * 4 + 1:_c * 4 + 2],
                                           b[:, 5, :], ALU.add, ALU.mult)
                    v.tensor_add(bscr1[:], bscr1[:], bscr2[:])
                    v.tensor_tensor(bscr1[:], bscr1[:], ps_r1[:, 0:512], ALU.mult)
                    v.scalar_tensor_tensor(bscr2[:], b[:, 2, :], carryC[:, _c * 4 + 2:_c * 4 + 3],
                                           b[:, 6, :], ALU.add, ALU.mult)
                    v.scalar_tensor_tensor(ascr_b := bsq[0][:], b[:, 3, :],
                                           carryC[:, _c * 4 + 3:_c * 4 + 4],
                                           b[:, 7, :], ALU.add, ALU.mult)
                    v.tensor_add(bscr2[:], bscr2[:], bsq[0][:])
                    v.tensor_tensor(bscr2[:], bscr2[:], ps_r2[:, 0:512], ALU.mult)
                    cV.inc(v.tensor_add(bl[:, _c, :], bscr1[:], bscr2[:]))

        if dbg and kstop == 4:
            with nc.Block() as blk:
                @blk.sync
                def _(sync):
                    for c in range(DC):
                        qS.inc(sync.dma_start(dbg8[0, c * 128:(c + 1) * 128, base:base + 512],
                                              bl[:, c, :]), 16)
                    sync.wait_ge(qS.sem, qS.v)
        if kstop <= 4:
            return

        with nc.Block() as blk:
            sb0 = cS.v
            tb0 = cT.v

            @blk.scalar
            def _(s):
                for c in range(DC):
                    if c >= 2:
                        s.wait_ge(cT.sem, tb0 + 2 * (c - 1))
                    cS.inc(s.activation(bsq[c % 2][:], bl[:, c, :], AF.Square))

            @blk.tensor
            def _(t):
                for c in range(DC):
                    t.wait_ge(cS.sem, sb0 + c + 1)
                    cT.inc(t.matmul(ps_s[0:1, 0:512], ones_col[:], bl[:, c, :],
                                    start=(c == 0), stop=(c == 7)))
                    cT.inc(t.matmul(ps_r1[0:1, 0:512], ones_col[:], bsq[c % 2][:],
                                    start=(c == 0), stop=(c == 7)))

            @blk.vector
            def _(v):
                v.wait_ge(cT.sem, tb0 + 16)
                v.tensor_scalar(growA[:], ps_s[0:1, 0:512], 1.0 / D, None, ALU.mult)
                v.tensor_scalar(growB[:], ps_r1[0:1, 0:512], 1.0 / D, None, ALU.mult)
                v.tensor_mul(growC[:], growA[:], growA[:])
                cV.inc(v.tensor_sub(growC[:], growC[:], growB[:]))

            @blk.scalar
            def _(s):
                s.wait_ge(cV.sem, cV.v)
                s.activation(growC[:], growC[:], AF.Ln, scale=-1.0, bias=epsc[0:1, 0:1])
                cS.inc(s.activation(growC[:], growC[:], AF.Exp, scale=-0.5))

            @blk.vector
            def _(v):
                v.wait_ge(cS.sem, cS.v)
                cV.inc(v.scalar_tensor_tensor(growD[:], growA[:], -1.0, growC[:],
                                              ALU.mult, ALU.mult))

            @blk.scalar
            def _(s):
                s.wait_ge(cV.sem, cV.v)
                s.copy(growAb[:], growC[:])
                cS.inc(s.copy(growBb[:], growD[:]))

            @blk.tensor
            def _(t):
                t.wait_ge(cS.sem, cS.v)
                cT.inc(t.matmul(ps_r1[:, 0:512], ones1[:], growAb[:], start=True, stop=True))
                cT.inc(t.matmul(ps_r2[:, 0:512], ones1[:], growBb[:], start=True, stop=True))

            @blk.vector
            def _(v):
                v.wait_ge(cT.sem, cT.v)
                for c in range(DC):
                    v.tensor_tensor(bscr1[:], bl[:, c, :], ps_r1[:, 0:512], ALU.mult)
                    cV.inc(v.tensor_tensor(nrm[:, c, :], bscr1[:], ps_r2[:, 0:512], ALU.add))

        with nc.Block() as blk:
            sb0 = cS.v
            tb0 = cT.v

            @blk.scalar
            def _(s):
                for s_i in range(NSUB):
                    sub = t_i * NSUB + s_i
                    cS.inc(s.activation(ln2tm[:, s_i, :], xb16[:, s_i, :], AF.Identity,
                                        scale=lnst[:, sub, 0:1], bias=lnst[:, sub, 1:2]))
                for s_i in range(NSUB):
                    s.wait_ge(cT.sem, tb0 + (s_i + 1) * 8)
                    for c in range(DC):
                        cS.inc(s.copy(ln1Fb[:, c, s_i * 128:(s_i + 1) * 128],
                                      ps_tm[:, c * 128:(c + 1) * 128]))

            @blk.tensor
            def _(t):
                for s_i in range(NSUB):
                    t.wait_ge(cS.sem, sb0 + (s_i + 1))
                    if s_i >= 1:
                        t.wait_ge(cS.sem, sb0 + NSUB + s_i * 8)
                    for c in range(DC):
                        cT.inc(t.transpose(ps_tm[:, c * 128:(c + 1) * 128],
                                           ln2tm[:, s_i, c * 128:(c + 1) * 128], identity[:]))

        if dbg and kstop == 5:
            with nc.Block() as blk:
                @blk.sync
                def _(sync):
                    for c in range(DC):
                        qS.inc(sync.dma_start(dbg8[0, c * 128:(c + 1) * 128, base:base + 512],
                                              nrm[:, c, :]), 16)
                        qS.inc(sync.dma_start(dbg8[1, c * 128:(c + 1) * 128, base:base + 512],
                                              ln1Fb[:, c, :]), 16)
                    sync.wait_ge(qS.sem, qS.v)
        if kstop <= 5:
            return

        with nc.Block() as blk:
            tb0 = cT.v
            vb0 = cV.v

            @blk.tensor
            def _(t):
                for fc in range(8):
                    pb = ps0 if fc % 2 == 0 else ps1
                    if fc >= 2:
                        t.wait_ge(cV.sem, vb0 + fc - 1)
                    for dc in range(8):
                        cT.inc(t.matmul(pb[:, 0:512], wo_chunk(dc, fc), nrm[:, dc, :],
                                        start=(dc == 0), stop=(dc == 7)))

            @blk.vector
            def _(v):
                for fc in range(8):
                    pb = ps0 if fc % 2 == 0 else ps1
                    v.wait_ge(cT.sem, tb0 + (fc + 1) * 8)
                    cV.inc(v.scalar_tensor_tensor(attF[:, fc, :], pb[:, 0:512],
                                                  bo2[:, fc:fc + 1], ln1Fb[:, fc, :],
                                                  ALU.add, ALU.add))

        for s_i in range(NSUB):
            sub = t_i * NSUB + s_i
            with nc.Block() as blk:
                @blk.tensor
                def _(t, _s=s_i):
                    for c in range(DC):
                        cT.inc(t.transpose(ps_tm[:, c * 128:(c + 1) * 128],
                                           attF[:, c, _s * 128:(_s + 1) * 128], identity[:]))

                @blk.scalar
                def _(s, _s=s_i):
                    s.wait_ge(cT.sem, cT.v)
                    s.activation(jkB[:], ps_tm[:], AF.Identity, accum_out=colsc[:, 0:1])
                    cS.inc(s.activation(jkB[:], ps_tm[:], AF.Square, accum_out=colsc[:, 1:2]))

                @blk.vector
                def _(v, _s=s_i):
                    cV.inc(v.tensor_copy(jkB[0:1, 1:2], ones1[0:1, 0:1]))
                    v.wait_ge(cS.sem, cS.v)
                    v.tensor_scalar(colsc[:, 0:2], colsc[:, 0:2], 1.0 / D, None, ALU.mult)
                    v.tensor_copy(srows[:, 0:8], srows[:, 8:16])
                    v.tensor_copy(srows[:, 8:16], srows[:, 0:8])
                    cV.inc(v.scalar_tensor_tensor(colsc[:, 2:3], colsc[:, 0:1], colsc[:, 0:1],
                                                  colsc[:, 1:2], ALU.mult, ALU.subtract))

                @blk.scalar
                def _(s, _s=s_i):
                    s.wait_ge(cV.sem, cV.v)
                    s.activation(colsc[:, 3:4], colsc[:, 2:3], AF.Ln, scale=-1.0,
                                 bias=epsc[:, 0:1])
                    cS.inc(s.activation(colsc[:, 3:4], colsc[:, 3:4], AF.Exp, scale=-0.5))

                @blk.vector
                def _(v, _s=s_i):
                    v.wait_ge(cS.sem, cS.v)
                    cV.inc(v.scalar_tensor_tensor(colsc[:, 4:5], colsc[:, 0:1], -1.0,
                                                  colsc[:, 3:4], ALU.mult, ALU.mult))

                @blk.scalar
                def _(s, _s=s_i):
                    s.wait_ge(cV.sem, cV.v)
                    cS.inc(s.activation(jkB[:], ps_tm[:], AF.Identity,
                                        scale=colsc[:, 3:4], bias=colsc[:, 4:5]))

                @blk.vector
                def _(v, _s=s_i):
                    v.wait_ge(cS.sem, cS.v)
                    cV.inc(v.tensor_add(x1t[:, _s, :], jkB[:], xb16[:, _s, :]))

                @blk.scalar
                def _(s, _s=s_i):
                    s.wait_ge(cV.sem, cV.v)
                    cS.inc(s.activation(jkB[:], x1t[:, _s, :], AF.Identity,
                                        accum_out=colsc[:, 5:6]))
                    cS.inc(s.activation(jkB[:], x1t[:, _s, :], AF.Square,
                                        accum_out=colsc[:, 6:7]))

                @blk.vector
                def _(v, _s=s_i):
                    v.wait_ge(cS.sem, cS.v)
                    v.tensor_scalar(colsc[:, 5:7], colsc[:, 5:7], 1.0 / D, None, ALU.mult)
                    v.tensor_copy(srows[:, 0:8], srows[:, 8:16])
                    v.tensor_copy(srows[:, 8:16], srows[:, 0:8])
                    cV.inc(v.scalar_tensor_tensor(colsc[:, 7:8], colsc[:, 5:6], colsc[:, 5:6],
                                                  colsc[:, 6:7], ALU.mult, ALU.subtract))

                @blk.scalar
                def _(s, _s=s_i):
                    s.wait_ge(cV.sem, cV.v)
                    s.activation(colsc[:, 6:7], colsc[:, 7:8], AF.Ln, scale=-1.0,
                                 bias=epsc[:, 0:1])
                    cS.inc(s.activation(colsc[:, 6:7], colsc[:, 6:7], AF.Exp, scale=-0.5))

                @blk.vector
                def _(v, _s=s_i):
                    v.wait_ge(cS.sem, cS.v)
                    cV.inc(v.scalar_tensor_tensor(colsc[:, 5:6], colsc[:, 5:6], -1.0,
                                                  colsc[:, 6:7], ALU.mult, ALU.mult))

                @blk.scalar
                def _(s, _s=s_i):
                    s.wait_ge(cV.sem, cV.v)
                    cS.inc(s.activation(ln2tm[:, _s, :], x1t[:, _s, :], AF.Identity,
                                        scale=colsc[:, 6:7], bias=colsc[:, 5:6]))

        if dbg and kstop == 6:
            with nc.Block() as blk:
                @blk.sync
                def _(sync):
                    for c in range(DC):
                        qS.inc(sync.dma_start(dbg8[0, c * 128:(c + 1) * 128, base:base + 512],
                                              attF[:, c, :]), 16)
                    for s_i in range(NSUB):
                        r0 = base + s_i * 128
                        qS.inc(sync.dma_start(dbgT[0, r0:r0 + 128, :], x1t[:, s_i, :]), 16)
                        qS.inc(sync.dma_start(dbgT[1, r0:r0 + 128, :], ln2tm[:, s_i, :]), 16)
                    sync.wait_ge(qS.sem, qS.v)
        if kstop <= 6:
            return

        with nc.Block() as blk:
            sb0 = cS.v
            tb0 = cT.v

            @blk.tensor
            def _(t):
                for s_i in range(NSUB):
                    if s_i >= 1:
                        t.wait_ge(cS.sem, sb0 + s_i * 8)
                    for c in range(DC):
                        cT.inc(t.transpose(ps_tm[:, c * 128:(c + 1) * 128],
                                           ln2tm[:, s_i, c * 128:(c + 1) * 128], identity[:]))

            @blk.scalar
            def _(s):
                for s_i in range(NSUB):
                    s.wait_ge(cT.sem, tb0 + (s_i + 1) * 8)
                    for c in range(DC):
                        cS.inc(s.copy(ln2F[:, c, s_i * 128:(s_i + 1) * 128],
                                      ps_tm[:, c * 128:(c + 1) * 128]))

        if dbg and kstop == 7:
            with nc.Block() as blk:
                @blk.sync
                def _(sync):
                    for c in range(DC):
                        qS.inc(sync.dma_start(dbg8[0, c * 128:(c + 1) * 128, base:base + 512],
                                              ln2F[:, c, :]), 16)
                    sync.wait_ge(qS.sem, qS.v)
        if kstop <= 7:
            return

        with nc.Block() as blk:
            sb0 = cS.v
            tb0 = cT.v

            @blk.tensor
            def _(t):
                for fc in range(16):
                    pb = ps0 if fc % 2 == 0 else ps1
                    if fc >= 2:
                        t.wait_ge(cS.sem, sb0 + fc - 1)
                    for dc in range(8):
                        cT.inc(t.matmul(pb[:, 0:512], wq_chunk(dc, fc), ln2F[:, dc, :],
                                        start=(dc == 0), stop=(dc == 7)))

            @blk.scalar
            def _(s):
                for fc in range(16):
                    pb = ps0 if fc % 2 == 0 else ps1
                    s.wait_ge(cT.sem, tb0 + (fc + 1) * 8)
                    cS.inc(s.activation(qF[:, fc, :], pb[:, 0:512], AF.Identity,
                                        bias=bias_q[:, fc:fc + 1]))

        if dbg and kstop == 8:
            with nc.Block() as blk:
                @blk.sync
                def _(sync):
                    for fc in range(16):
                        qS.inc(sync.dma_start(dbgQ[fc * 128:(fc + 1) * 128, base:base + 512],
                                              qF[:, fc, :]), 16)
                    sync.wait_ge(qS.sem, qS.v)
        if kstop <= 8:
            return

        for s_i in range(NSUB):
            emit_ffn_sub(t_i, s_i)

    if kstop >= 4:
        for t_i in range(NT):
            emit_passB_tile(t_i)

    with nc.Block() as blk:
        @blk.sync
        def _(sync):
            sync.wait_ge(qS.sem, qS.v)
            sync.wait_ge(qG.sem, qG.v)

    ctx.close()
    return nc


def _stage(inputs, T):
    x = np.asarray(inputs["L0_attn_x"])
    B, L, _ = x.shape
    xf = np.ascontiguousarray(x.reshape(B * L, D)).astype(np.float32, copy=False)
    pos = np.asarray(inputs["L0_attn_pos_phases"], np.float32)
    posTf = np.ascontiguousarray(pos[:L].T)
    cores_per_seq = NCORE // B

    def col(b):
        b = np.asarray(b, np.float32).reshape(-1)
        n = b.shape[0] // 128
        return np.ascontiguousarray(b.reshape(n, 128).T)

    wcomb = np.empty((NEXP, 2 * D), np.uint16)
    wcomb[:, :D] = np.asarray(inputs["L0_ffn_W_down"], np.float32).view(np.uint16)[:, 1::2]
    wcomb[:, D:] = np.asarray(inputs["L0_ffn_W_up"], np.float32).view(np.uint16)[:, 1::2]
    wcomb = wcomb.view(ml_dtypes.bfloat16)

    base_common = dict(
        wpv=np.asarray(inputs["L0_attn_Wpv"], np.float32),
        wk=np.asarray(inputs["L0_attn_Wk"], np.float32),
        wkv=np.asarray(inputs["L0_attn_Wkv"], np.float32),
        wg1=np.asarray(inputs["L0_attn_Wg1"], np.float32),
        wg2c=col(inputs["L0_attn_Wg2"]),
        wb1=np.asarray(inputs["L0_attn_Wb1"], np.float32),
        wb2=np.asarray(inputs["L0_attn_Wb2"], np.float32),
        wo=np.asarray(inputs["L0_attn_Wo"], np.float32),
        wq=np.asarray(inputs["L0_ffn_Wq"], np.float32),
        bpvc=col(inputs["L0_attn_bpv"]), bkc=col(inputs["L0_attn_bk"]),
        bkvc=col(inputs["L0_attn_bkv"]), bg1c=col(inputs["L0_attn_bg1"]),
        bg2r=np.asarray(inputs["L0_attn_bg2"], np.float32).reshape(1, 1),
        bb1c=col(inputs["L0_attn_bb1"]),
        bb2r=np.asarray(inputs["L0_attn_bb2"], np.float32).reshape(2, 1),
        boc=col(inputs["L0_attn_bo"]), bqc=col(inputs["L0_ffn_bq"]),
        gc_in=col(inputs["L0_attn_ln_g"]), lnbc=col(inputs["L0_attn_ln_b"]),
        k1t=np.ascontiguousarray(np.asarray(inputs["L0_ffn_K1"], np.float32).T),
        k2t=np.ascontiguousarray(np.asarray(inputs["L0_ffn_K2"], np.float32).T),
        wcomb=wcomb,
    )

    import time as _t
    nonce = (int(_t.time_ns()) % 2000000000) | 1
    in_maps = []
    for c in range(NCORE):
        seq = c // cores_per_seq
        part = c % cores_per_seq
        g0 = seq * L + part * T
        xbv = np.empty((T + 1, D), np.float32)
        xbv[0] = 0.0 if part == 0 else xf[g0 - 1]
        xbv[1:] = xf[g0:g0 + T]
        m = dict(base_common)
        m["xb"] = xbv
        m["posT"] = np.ascontiguousarray(posTf[:, part * T:part * T + T])
        m["bflag"] = np.full((128, 1), 0.0 if part == 0 else 1.0, np.float32)
        if part == 2:
            rpart = 0
        elif part == 3:
            rpart = 1
        else:
            rpart = None
        xrv = np.zeros((T + 1, D), np.float32)
        prv = np.zeros((D, T), np.float32)
        rfl = 0.0
        if rpart is not None:
            rg0 = seq * L + rpart * T
            xrv[0] = 0.0 if rpart == 0 else xf[rg0 - 1]
            xrv[1:] = xf[rg0:rg0 + T]
            prv = np.ascontiguousarray(posTf[:, rpart * T:rpart * T + T])
            rfl = 0.0 if rpart == 0 else 1.0
        m["xr"] = xrv
        m["posTr"] = prv
        m["rflag"] = np.full((128, 1), rfl, np.float32)
        m["metai"] = np.array([[part % 2, nonce]], np.int32)
        w = {0: (0.0, 0.0, 0.0), 1: (1.0, 0.0, 0.0),
             2: (0.0, 1.0, 1.0), 3: (1.0, 1.0, 1.0)}[part]
        cm = np.zeros((128, 4), np.float32)
        cm[:, 0], cm[:, 1], cm[:, 2] = w
        m["cmask"] = cm
        in_maps.append(m)
    return in_maps


LAST_RES = None


def kernel(**inputs):
    global LAST_RES
    x = np.asarray(inputs["L0_attn_x"])
    B, L, _ = x.shape
    T = (B * L) // NCORE
    nc = build_program(T)
    in_maps = _stage(inputs, T)
    res = run_bass_kernel_spmd(nc, in_maps, core_ids=list(range(NCORE)))
    LAST_RES = res
    outs = [np.asarray(r["out"]) for r in res.results]
    return np.concatenate(outs, axis=0).reshape(B, L, D).astype(np.float32)
